# version 01
# baseline (speedup 1.0000x reference)
import contextlib
import os as _os2
import numpy as np
import ml_dtypes
import concourse.bass as bass
import concourse.mybir as mybir
from concourse.bass_utils import run_bass_kernel_spmd

F32 = mybir.dt.float32
BF16 = mybir.dt.bfloat16
FP8 = mybir.dt.float8e4
I8 = mybir.dt.int8
AF = mybir.ActivationFunctionType
ALU = mybir.AluOpType
bfdt = ml_dtypes.bfloat16

L = 4096; D = 1024; DI = 2048; NH = 32; HD = 64; DS = 128
T = 128; NCC = L // T
SPANC = 2; SPAN = SPANC * T
EPS = 1e-5
INV7P5 = 1.0 / 7.5


class Ctr:
    def __init__(self, stk, nc, name, step=1, skipv=False):
        self.sem = stk.enter_context(nc.semaphore(name))
        self.val = 0
        self.step = step
        self.name = name
        self.skipv = skipv
        self.dirty = False

    def inc(self, instr):
        instr.then_inc(self.sem, self.step, skip_validation=(self.step not in (1, 16)) or self.skipv)
        self.val += self.step
        self.dirty = True
        return (self, self.val)


PE_CTR = []


def edge(ctr, eng, tok):
    c, v = tok
    eng.wait_ge(c.sem, v)


class Buf:
    def __init__(self, name="", exclusive=False):
        self.w = None
        self.r = []
        self.name = name
        self.exclusive = exclusive

    def bw(self, ctr, eng):
        for tok in ([self.w] if self.w else []) + self.r:
            edge(ctr, eng, tok)

    def aw(self, tok):
        self.w = tok
        self.r = []

    def br(self, ctr, eng):
        if self.exclusive:
            for tok in ([self.w] if self.w else []) + self.r:
                edge(ctr, eng, tok)
            return
        if self.w:
            edge(ctr, eng, self.w)

    def ar(self, tok):
        self.r.append(tok)


def bc3(apin, shape):
    return apin.unsqueeze(2).broadcast_to(shape)


def build(n_chunks=NCC, debug_taps=(), p1_only=False, front_only=False):
    nc = bass.Bass("TRN2", target_bir_lowering=False)
    NCH = n_chunks

    u_d = nc.dram_tensor("u", [L, D], F32, kind="ExternalInput").ap()
    WinT_d = nc.dram_tensor("WinT", [D, 4672], F32, kind="ExternalInput").ap()
    WoT_d = nc.dram_tensor("WoT", [DI, D], F32, kind="ExternalInput").ap()
    WfcDT_d = nc.dram_tensor("WfcDT", [DI, NH], F32, kind="ExternalInput").ap()
    Wf_d = nc.dram_tensor("Wf", [D, D], F32, kind="ExternalInput").ap()
    convw_d = nc.dram_tensor("convw", [128, 20, 4], F32, kind="ExternalInput").ap()
    convb_d = nc.dram_tensor("convb", [128, 20], F32, kind="ExternalInput").ap()
    dtb_d = nc.dram_tensor("dtb", [NH], F32, kind="ExternalInput").ap()
    Alog_d = nc.dram_tensor("Alog", [NH], F32, kind="ExternalInput").ap()
    Dv_d = nc.dram_tensor("Dv", [NH], F32, kind="ExternalInput").ap()
    normw_d = nc.dram_tensor("normw", [128, 16], F32, kind="ExternalInput").ap()
    fbias_d = nc.dram_tensor("fbias", [D], F32, kind="ExternalInput").ap()
    out_d = nc.dram_tensor("out", [L, D], F32, kind="ExternalOutput").ap()

    dbg_specs = {
        "x1": ([L, D], F32), "h2": ([L, D], F32), "dt": ([L, 64], F32),
        "xBCs": ([128, 20, L], F32), "Yf": ([128, 16, L], F32),
        "Yb": ([128, 16, L], F32), "y": ([128, 16, L], F32),
        "yn": ([128, 16, L], F32), "x2": ([L, D], F32),
        "h1": ([L, D], F32), "xd": ([L, D], F32), "wdT": ([128, 8, D], F32),
        "sm": ([128, 32], F32),
    }
    dbg = {}
    for name in debug_taps:
        shp, dt_ = dbg_specs[name]
        dbg[name] = nc.dram_tensor("dbg_" + name, shp, dt_, kind="ExternalOutput").ap()

    sp_x1 = nc.dram_tensor("sp_x1", [L, D], F32, kind="Internal").ap()
    sp_h2T = nc.dram_tensor("sp_h2T", [128, 8, L], BF16, kind="Internal").ap()
    sp_xog = nc.dram_tensor("sp_xog", [L, DI], BF16, kind="Internal").ap()
    sp_xogT = nc.dram_tensor("sp_xogT", [128, 16, L], BF16, kind="Internal").ap()
    sp_BCb = nc.dram_tensor("sp_BCb", [128, 2, L], BF16, kind="Internal").ap()
    sp_dt = nc.dram_tensor("sp_dt", [L, 64], F32, kind="Internal").ap()
    sp_fdT = nc.dram_tensor("sp_fdT", [NH, L], BF16, kind="Internal").ap()
    sp_flush = nc.dram_tensor("sp_flush", [16, 128, 4096], BF16, kind="Internal").ap()
    sp_Yf = nc.dram_tensor("sp_Yf", [128, 16, L + 1], BF16, kind="Internal").ap()
    bn_cs = nc.dram_tensor("bn_cs", [4, 16384], BF16, kind="Internal").ap()
    bn_misc = nc.dram_tensor("bn_misc", [64], F32, kind="Internal").ap()

    ii = np.arange(T)
    c_csf = nc.inline_tensor((ii[:, None] <= ii[None, :]).astype(np.float32), "c_csf").ap()
    c_csb = nc.inline_tensor((ii[:, None] >= ii[None, :]).astype(np.float32), "c_csb").ap()
    c_dsf = nc.inline_tensor((ii[:, None] > ii[None, :]).astype(np.float32), "c_dsf").ap()
    c_dsb = nc.inline_tensor((ii[:, None] < ii[None, :]).astype(np.float32), "c_dsb").ap()
    c_ones = nc.inline_tensor(np.ones((T, T), np.float32), "c_ones").ap()
    c_mgf = nc.inline_tensor(np.asarray(ii[None, :] >= ii[:, None], bfdt), "c_mgf").ap()
    c_mgb = nc.inline_tensor(np.asarray(ii[None, :] <= ii[:, None], bfdt), "c_mgb").ap()
    c_idf = nc.inline_tensor(np.eye(T, dtype=np.float32), "c_idf").ap()
    c_idb = nc.inline_tensor(np.eye(T, dtype=bfdt), "c_idb").ap()
    c_nbi = nc.inline_tensor((np.eye(T, dtype=np.float32) * -1e30).astype(bfdt), "c_nbi").ap()
    c_utf = nc.inline_tensor(np.tile((ii[None, :] < ii[:, None]).astype(bfdt), (1, 4)), "c_utf").ap()
    c_utb = nc.inline_tensor(np.tile((ii[None, :] > ii[:, None]).astype(bfdt), (1, 4)), "c_utb").ap()
    rhsS_np = np.zeros((128, NH * T), bfdt)
    for h in range(NH):
        rhsS_np[h, h * T:(h + 1) * T] = -1.0
        rhsS_np[NH + h, h * T:(h + 1) * T] = -1.0
    c_rhsS = nc.inline_tensor(rhsS_np, "c_rhsS").ap()
    c_zc16 = nc.inline_tensor(np.zeros((128, 16), bfdt), "c_zc16").ap()
    c_ones2b = nc.inline_tensor(np.ones((2, 128), bfdt), "c_ones2b").ap()

    stk = contextlib.ExitStack()
    sbt = lambda name, shp, dt_: stk.enter_context(nc.sbuf_tensor(name, shp, dt_))
    pst = lambda name, shp, dt_: stk.enter_context(nc.psum_tensor(name, shp, dt_))

    wdT = sbt("wdT", [128, 8, D], BF16)
    s_csf = sbt("s_csf", [128, 128], F32); s_csb = sbt("s_csb", [128, 128], F32)
    s_dsf = sbt("s_dsf", [128, 128], F32); s_dsb = sbt("s_dsb", [128, 128], F32)
    s_ones = sbt("s_ones", [128, 128], F32)
    s_mgf = sbt("s_mgf", [128, 128], BF16); s_mgb = sbt("s_mgb", [128, 128], BF16)
    s_idf = sbt("s_idf", [128, 128], F32); s_idb = sbt("s_idb", [128, 128], BF16)
    s_nbi = sbt("s_nbi", [128, 128], BF16)
    s_utf = sbt("s_utf", [128, 512], BF16); s_utb = sbt("s_utb", [128, 512], BF16)
    s_rhsS = sbt("s_rhsS", [128, NH * T], BF16)
    s_dtbrep = sbt("s_dtbrep", [128, 64], F32)
    s_Arep = sbt("s_Arep", [128, NH], F32)
    s_Drep = sbt("s_Drep", [128, 16], F32)
    s_Dcol = sbt("s_Dcol", [NH, 1], F32)
    s_biasrep = sbt("s_biasrep", [128, D], BF16)
    lhsS = sbt("lhsS", [128, 4, 128], BF16)
    Sst = sbt("Sst", [128, DI], BF16)
    ybcar = sbt("ybcar", [128, 16], F32)
    onescol_b = sbt("onescol_b", [128, 1], BF16)
    sqrow_b = sbt("sqrow_b", [1, 128], BF16)
    s_eps = sbt("s_eps", [128, 2], F32)
    rowb_sb = sbt("rowb_sb", [1, 128], BF16)

    u_sb = sbt("u_sb", [128, 2, D], F32)
    x1_sb = sbt("x1_sb", [128, D], F32)
    h1_sb = sbt("h1_sb", [128, D], F32)
    junk = sbt("junk", [128, D], F32)
    q8_sb = sbt("q8_sb", [128, D], FP8)
    qi8_sb = sbt("qi8_sb", [128, D], I8)
    xd_sb = sbt("xd_sb", [128, D], BF16)
    xdT_sb = sbt("xdT_sb", [128, 8, 128], BF16)
    h2b_sb = sbt("h2b_sb", [128, D], BF16)
    h2T_sb = sbt("h2T_sb", [128, 1, 8, 128], BF16)
    dt_sb = sbt("dt_sb", [128, 2, 64], F32)
    xog_sb = sbt("xog_sb", [128, 1, DI], BF16)
    xdt_sb = sbt("xdt_sb", [128, DI], BF16)
    xdtds_sb = sbt("xdtds_sb", [128, DI], BF16)
    L_sb = sbt("L_sb", [128, NH * T], BF16)
    G_sb = sbt("G_sb", [128, 128], BF16)
    Blh_sb = sbt("Blh_sb", [128, 128], BF16)
    csS_sb = sbt("csS_sb", [128, NH], F32)
    ds_sb = sbt("ds_sb", [128, NH], F32)
    dtds_sb = sbt("dtds_sb", [128, NH], F32)
    a_sb = sbt("a_sb", [128, NH], F32)
    ecrep_sb = sbt("ecrep_sb", [128, NH], F32)
    ecsrep_sb = sbt("ecsrep_sb", [128, 16, 128], BF16)
    ecsT_sb = sbt("ecsT_sb", [32, 128], BF16)
    Y_sb = sbt("Y_sb", [128, 16, 128], BF16)
    tmp_sb = sbt("tmp_sb", [128, 16, 128], BF16)
    tmpD_sb = sbt("tmpD_sb", [128, 16, 128], BF16)
    sm_sb = sbt("sm_sb", [128, 16], F32)
    sm2_sb = sbt("sm2_sb", [128, 16], F32)
    arow_sb = sbt("arow_sb", [1, 128], F32)
    qam = sbt("qam", [128, 32], F32)
    qsc = sbt("qsc", [128, 32], F32)
    qiv = sbt("qiv", [128, 32], F32)
    qs6 = sbt("qs6", [128, 32], F32)

    PA = pst("PA", [128, 1024], F32)
    PB = pst("PB", [128, 1024], F32)
    PS = pst("PS", [128, 1024], F32)
    PY1 = pst("PY1", [128, 512], F32)
    PY2 = pst("PY2", [128, 512], F32)
    bufPA, bufPB, bufPS = Buf("PA", True), Buf("PB", True), Buf("PS", True)
    bufPY1, bufPY2 = Buf("PY1", True), Buf("PY2", True)

    cT = Ctr(stk, nc, "cT"); cV = Ctr(stk, nc, "cV"); cS = Ctr(stk, nc, "cS")
    PE_CTR.clear()
    cG = Ctr(stk, nc, "cG")
    PE_CTR.append(cT)


    EV = nc.vector; ES = nc.scalar; ET = nc.tensor; EG = nc.gpsimd; EQ = nc.sync

    bufs = {}
    def B(name):
        if name not in bufs:
            bufs[name] = Buf(name)
        return bufs[name]

    dma_ctrs = {}
    cDALL = Ctr(stk, nc, "cDALL", 16)
    import os as _os8
    MAXQ = int(_os8.environ.get("MKMAXQ", "0"))
    DMABAR = int(_os8.environ.get("MKDMABAR", "1"))

    def chunk_dma_barrier():
        pass

    def dma(out, in_, dst=(), src=(), q=None, qc=None, transpose=False, slow=False, ks=""):
        q = q or EQ
        serialize(q, None)
        key = (dst[0].name if dst else "st_" + out.tensor.name + "_" + (src[0].name if src else "")) + ("_T" if transpose else "") + ks
        if key not in dma_ctrs:
            dma_ctrs[key] = Ctr(stk, nc, "cD_" + key, 1 if transpose else 16,
                                skipv=transpose)
        qc = dma_ctrs[key]
        for b_ in dst: b_.bw(qc, q)
        for b_ in src: b_.br(qc, q)
        if MAXQ and cDALL.val > MAXQ * 16:
            q.wait_ge(cDALL.sem, cDALL.val - MAXQ * 16)
        if transpose:
            ins = q.dma_start_transpose(out, in_)
            tokt = qc.inc(ins)
            q.wait_ge(tokt[0].sem, tokt[1])
            fkey = key + "_F"
            if fkey not in dma_ctrs:
                dma_ctrs[fkey] = Ctr(stk, nc, "cD_" + fkey, 16)
                dma_ctrs[fkey].slot = len([k for k in dma_ctrs if k.endswith("_F")]) - 1
            qc = dma_ctrs[fkey]
            slot = qc.slot
            n = out.free_size()
            shp = out.shape
            if len(shp) == 2:
                fout = sp_flush[slot, :, 0:n]
            else:
                fout = sp_flush[slot, :, 0:n].rearrange("p (a b) -> p a b", a=shp[1])
            ins = q.dma_start(out=fout, in_=out)
        elif slow:
            ins = q.dma_start(out=out, in_=in_, allow_slow_non_contiguous=True)
        else:
            ins = q.dma_start(out=out, in_=in_)
        tok = qc.inc(ins)
        for b_ in dst: b_.aw(tok)
        for b_ in src: b_.ar(tok)
        return tok

    def drain_dmas(eng):
        for c in dma_ctrs.values():
            eng.wait_ge(c.sem, c.val)

    import os as _os7
    MKSER = int(_os7.environ.get("MKSER", "0"))

    def serialize(eng, ctr):
        if not MKSER:
            return
        if ctr is None:
            if MKSER & 4:
                for c in (cT, cV, cS, cG):
                    if c.val:
                        eng.wait_ge(c.sem, c.val)
            return
        if MKSER & 1:
            for c in (cT, cV, cS, cG):
                if c is not ctr and c.val:
                    eng.wait_ge(c.sem, c.val)
        if MKSER & 2:
            for c in dma_ctrs.values():
                eng.wait_ge(c.sem, c.val)
        if (MKSER & 64) and ctr is cV:
            for c in dma_ctrs.values():
                eng.wait_ge(c.sem, c.val)
        if (MKSER & 128) and ctr is cS:
            for c in dma_ctrs.values():
                eng.wait_ge(c.sem, c.val)
        if (MKSER & 256) and ctr is cT:
            for c in dma_ctrs.values():
                eng.wait_ge(c.sem, c.val)
        if (MKSER & 512) and ctr is cG:
            for c in dma_ctrs.values():
                eng.wait_ge(c.sem, c.val)
        if (MKSER & 1024) and ctr is cV:
            for k, c in dma_ctrs.items():
                if k.endswith("_T") or k.endswith("_T_F"):
                    eng.wait_ge(c.sem, c.val)
        if (MKSER & 2048) and ctr is cV:
            for k, c in dma_ctrs.items():
                if not (k.endswith("_T") or k.endswith("_T_F")):
                    eng.wait_ge(c.sem, c.val)
        if (MKSER & 4096) and ctr is cV:
            for k, c in dma_ctrs.items():
                if k.startswith("st_"):
                    eng.wait_ge(c.sem, c.val)
        if (MKSER & 16384) and ctr is cV:
            for k, c in dma_ctrs.items():
                if any(s in k for s in ("ecsrep", "fdrep", "rhsSd", "bnc", "bne")):
                    eng.wait_ge(c.sem, c.val)
        if (MKSER & 8192) and ctr is cV:
            for k, c in dma_ctrs.items():
                if not k.startswith("st_") and not k.endswith("_T") and not k.endswith("_T_F"):
                    eng.wait_ge(c.sem, c.val)
        if (MKSER & 8) and ctr is cS and cT.val:
            eng.wait_ge(cT.sem, cT.val)
        if (MKSER & 16) and ctr is cV and cT.val:
            eng.wait_ge(cT.sem, cT.val)
        if (MKSER & 32) and ctr is cT:
            for c in (cS, cV):
                if c.val:
                    eng.wait_ge(c.sem, c.val)

    PSUM_BUFS = (bufPA, bufPB, bufPS, bufPY1, bufPY2)

    def op(eng, ctr, fn, dst=(), src=()):
        serialize(eng, ctr)

        for b_ in dst: b_.bw(ctr, eng)
        for b_ in src: b_.br(ctr, eng)
        ins = fn()
        tok = ctr.inc(ins)
        for b_ in dst: b_.aw(tok)
        for b_ in src: b_.ar(tok)
        return tok

    def mms(psum_ap, pairs, dst=(), src=()):
        serialize(ET, cT)
        for b_ in dst: b_.bw(cT, ET)
        for b_ in src: b_.br(cT, ET)
        n = len(pairs)
        ins = None
        for i, (lh, rh) in enumerate(pairs):
            ins = ET.matmul(psum_ap, lh, rh, start=(i == 0), stop=(i == n - 1))
        tok = cT.inc(ins)
        for b_ in dst: b_.aw(tok)
        for b_ in src: b_.ar(tok)
        return tok

    def pe_transpose(outap, inap, ident, dst=(), src=()):
        for b_ in dst: b_.bw(cT, ET)
        for b_ in src: b_.br(cT, ET)
        ET.transpose(outap, inap, ident)
        ins = ET.matmul(PS[0:1, 1016:1024], s_ones[0:1, 0:1], s_ones[0:1, 0:8],
                        start=True, stop=True)
        tok = cT.inc(ins)
        for b_ in dst: b_.aw(tok)
        for b_ in src: b_.ar(tok)
        return tok

    def mm_list(entries, dst=(), src=()):
        for b_ in dst: b_.bw(cT, ET)
        for b_ in src: b_.br(cT, ET)
        ins = None
        for (pa, lh, rh) in entries:
            ins = ET.matmul(pa, lh, rh, start=True, stop=True)
        tok = cT.inc(ins)
        for b_ in dst: b_.aw(tok)
        for b_ in src: b_.ar(tok)
        return tok

    def ln_stats(src_ap, scol, dsrc, tag):
        import os as _os
        CUT2 = int(_os.environ.get("MKCUT2", "99"))
        op(EV, cV, lambda: EV.reduce_sum(sm_sb[:, scol:scol + 1], src_ap, axis=mybir.AxisListType.X),
           dst=[B("sm" + tag)], src=dsrc)
        if CUT2 <= 1:
            return sm_sb[:, scol:scol + 1], sm_sb[:, scol:scol + 1], [B("sm" + tag)]
        op(ES, cS, lambda: ES.activation(junk[:, :], src_ap, AF.Square),
           dst=[B("junk")], src=dsrc)
        op(EV, cV, lambda: EV.reduce_sum(sm2_sb[:, scol:scol + 1], junk[:, :],
                                         axis=mybir.AxisListType.X),
           dst=[B("sm2" + tag)], src=[B("junk")])
        if CUT2 <= 2:
            return sm_sb[:, scol:scol + 1], sm_sb[:, scol:scol + 1], [B("sm2" + tag)]
        op(EV, cV, lambda: EV.tensor_scalar(sm_sb[:, scol + 1:scol + 2], sm_sb[:, scol:scol + 1],
                                            1.0 / D, None, ALU.mult),
           dst=[B("smm" + tag)], src=[B("sm" + tag)])
        op(EV, cV, lambda: EV.tensor_tensor(sm_sb[:, scol + 2:scol + 3], sm_sb[:, scol + 1:scol + 2],
                                            sm_sb[:, scol + 1:scol + 2], ALU.mult),
           dst=[B("smq" + tag)], src=[B("smm" + tag)])
        op(EV, cV, lambda: EV.scalar_tensor_tensor(sm_sb[:, scol + 3:scol + 4], sm2_sb[:, scol:scol + 1],
                                                   1.0 / D, sm_sb[:, scol + 2:scol + 3], ALU.mult, ALU.subtract),
           dst=[B("smv" + tag)], src=[B("sm2" + tag), B("smq" + tag)])
        if CUT2 <= 3:
            return sm_sb[:, scol + 1:scol + 2], sm_sb[:, scol + 2:scol + 3], [B("smv" + tag)]
        op(ES, cS, lambda: ES.activation(sm2_sb[:, scol + 1:scol + 2], sm_sb[:, scol + 3:scol + 4],
                                         AF.Ln, bias=s_eps[:, 0:1]),
           dst=[B("sml" + tag)], src=[B("smv" + tag), B("s_eps")])
        op(ES, cS, lambda: ES.activation(sm2_sb[:, scol + 2:scol + 3], sm2_sb[:, scol + 1:scol + 2],
                                         AF.Exp, scale=-0.5),
           dst=[B("smr" + tag)], src=[B("sml" + tag)])
        return sm_sb[:, scol + 1:scol + 2], sm2_sb[:, scol + 2:scol + 3], [B("smr" + tag), B("smm" + tag)]

    def quant(hsrc_ap, hbufs, outbf_ap, outbuf, tag):
        op(EV, cV, lambda: EV.reduce_max(qam[:, :], hsrc_ap.rearrange("p (a b) -> p a b", b=32),
                                         axis=mybir.AxisListType.X, apply_absolute_value=True),
           dst=[B("qam")], src=hbufs)
        op(EV, cV, lambda: EV.tensor_scalar(qsc[:, :], qam[:, :], INV7P5, None, ALU.mult),
           dst=[B("qsc0")], src=[B("qam")])
        op(EV, cV, lambda: EV.tensor_scalar(qsc[:, :], qsc[:, :], 1e-30, None, ALU.max),
           dst=[B("qsc1")], src=[B("qsc0")])
        op(EV, cV, lambda: EV.reciprocal(qiv[:, :], qsc[:, :]),
           dst=[B("qiv0")], src=[B("qsc1")])
        op(EV, cV, lambda: EV.tensor_scalar(qiv[:, :], qiv[:, :], 2.0 ** -6, None, ALU.mult),
           dst=[B("qsc")], src=[B("qiv0")])
        op(EV, cV, lambda: EV.tensor_scalar(qs6[:, :], qsc[:, :], 2.0 ** 6, None, ALU.mult),
           dst=[B("qs6")], src=[B("qsc1")])
        op(EV, cV, lambda: EV.tensor_tensor(q8_sb[:, :].rearrange("p (a b) -> p a b", b=32),
                                            hsrc_ap.rearrange("p (a b) -> p a b", b=32),
                                            bc3(qiv[:, :], [128, 32, 32]), ALU.mult),
           dst=[B("q8")], src=hbufs + [B("qsc")])
        op(EG, cG, lambda: EG.tensor_copy(qi8_sb[:, :], q8_sb[:, :].bitcast(I8)),
           dst=[B("qi8")], src=[B("q8")])
        return op(EV, cV, lambda: EV.tensor_tensor(outbf_ap.rearrange("p (a b) -> p a b", b=32),
                                                   qi8_sb[:, :].bitcast(FP8).rearrange("p (a b) -> p a b", b=32),
                                                   bc3(qs6[:, :], [128, 32, 32]), ALU.mult),
                  dst=[outbuf], src=[B("qi8"), B("qs6")])

    def ssd(k, d, par, xT_ap, xT_buf, BT_ap, CT_ap, BC_buf, x_tc_ap, x_tc_buf, scut=99):
        slot = 2 * (k % 2) + d
        cs_lh = s_csf if d == 0 else s_csb
        dsl = s_dsf if d == 0 else s_dsb
        mgG = s_mgf if d == 0 else s_mgb
        utm = s_utf if d == 0 else s_utb
        dtap = dt_sb[:, par, d * 32:(d + 1) * 32]
        dtbuf = B(f"dt{par}")

        op(EV, cV, lambda: EV.tensor_tensor(a_sb[:, :], dtap, s_Arep[:, :], ALU.mult),
           dst=[B("a")], src=[dtbuf, B("s_Arep")])
        mms(PS[:, 32:64], [(dsl[:, :], a_sb[:, :])], dst=[bufPS], src=[B("a")])
        mms(PS[:, 64:96], [(s_ones[:, :], a_sb[:, :])], dst=[bufPS], src=[B("a")])
        if scut == 11:
            return None
        op(ES, cS, lambda: ES.activation(ds_sb[:, :], PS[:, 32:64], AF.Exp),
           dst=[B("ds")], src=[bufPS])
        op(ES, cS, lambda: ES.activation(ecrep_sb[:, :], PS[:, 64:96], AF.Exp),
           dst=[B("ecrep")], src=[bufPS])
        if scut == 12:
            return None
        mms(PS[0:32, 96:224], [(a_sb[:, :], cs_lh[:, :])], dst=[bufPS], src=[B("a")])
        if scut == 121:
            return None
        op(ES, cS, lambda: ES.copy(lhsS[0:32, slot, :], PS[0:32, 96:224]),
           dst=[B(f"lhsSd{slot}")], src=[bufPS])
        if scut == 122:
            return None
        op(EV, cV, lambda: EV.scalar_tensor_tensor(lhsS[32:64, slot, :], PS[0:32, 96:224],
                                                   1.0, lhsS[0:32, slot, :], ALU.mult, ALU.subtract),
           dst=[B(f"lhsSe{slot}")], src=[bufPS, B(f"lhsSd{slot}")])
        if scut == 123:
            return None
        op(ES, cS, lambda: ES.activation(ecsT_sb[:, :], PS[0:32, 96:224], AF.Exp),
           dst=[B("ecsT")], src=[bufPS])
        if scut == 13:
            return None
        dma(bn_cs[slot, 0:8192].rearrange("(a b) -> a b", a=64), lhsS[0:64, slot, :],
            src=[B(f"lhsSd{slot}"), B(f"lhsSe{slot}")], dst=[B(f"bnc{slot}")])
        dma(s_rhsS[64 + 2 * slot:66 + 2 * slot, :],
            bn_cs[slot, 0:8192].rearrange("(a b) -> a b", a=2),
            src=[B(f"bnc{slot}")], dst=[B(f"rhsSd{slot}")])
        dma(bn_cs[slot, 8192:12288].rearrange("(a b) -> a b", a=32), ecsT_sb[:, :],
            src=[B("ecsT")], dst=[B(f"bne{slot}")])
        if scut == 14:
            return None
        for half in range(2):
            eb = bass.AP(tensor=bn_cs.tensor, offset=slot * 16384 + 8192 + half * T,
                         ap=[[0, 64], [2 * T, 16], [1, T]])
            dma(ecsrep_sb[half * 64:(half + 1) * 64, :, :], eb,
                dst=[B("ecsrep")], src=[B(f"bne{slot}")])
        if scut <= 1:
            return None
        op(EV, cV, lambda: EV.tensor_tensor(dtds_sb[:, :], dtap, ds_sb[:, :], ALU.mult),
           dst=[B("dtds")], src=[dtbuf, B("ds")])
        op(EV, cV, lambda: EV.tensor_tensor(xdt_sb[:, :].rearrange("t (h p) -> t h p", p=HD),
                                            x_tc_ap.rearrange("t (h p) -> t h p", p=HD),
                                            bc3(dtap, [128, 32, HD]), ALU.mult),
           dst=[B("xdt")], src=[x_tc_buf, dtbuf])
        op(EG, cG, lambda: EG.tensor_tensor(xdtds_sb[:, :].rearrange("t (h p) -> t h p", p=HD),
                                            x_tc_ap.rearrange("t (h p) -> t h p", p=HD),
                                            bc3(dtds_sb[:, :], [128, 32, HD]), ALU.mult),
           dst=[B("xdtds")], src=[x_tc_buf, B("dtds")])
        mms(PS[:, 224:352], [(BT_ap, s_idb[:, :])], dst=[bufPS], src=[BC_buf, B("s_idb")])
        op(ES, cS, lambda: ES.copy(Blh_sb[:, :], PS[:, 224:352]),
           dst=[B("Blh")], src=[bufPS])
        mms(PS[:, 352:480], [(BT_ap, CT_ap)], dst=[bufPS], src=[BC_buf])
        op(EV, cV, lambda: EV.tensor_tensor(G_sb[:, :], PS[:, 352:480], mgG[:, :], ALU.mult),
           dst=[B("G")], src=[bufPS, B(f"s_mg{d}")])
        if scut <= 2:
            return None
        for g in range(4):
            bufPS.bw(cT, ET)
            B(f"rhsSd{slot}").br(cT, ET)
            B(f"lhsSd{slot}").br(cT, ET)
            B(f"lhsSe{slot}").br(cT, ET)
            ins = None
            for i in range(2):
                mm_dst = PS[:, i * 512:(i + 1) * 512]
                ET.matmul(mm_dst, lhsS[:, slot, :],
                          s_rhsS[:, g * 1024 + i * 512: g * 1024 + (i + 1) * 512],
                          start=True, stop=False)
                ins = ET.matmul(mm_dst, s_nbi[:, :], utm[:, :], start=False, stop=True)
            tok = cT.inc(ins)
            bufPS.aw(tok)
            B(f"rhsSd{slot}").ar(tok); B(f"lhsSd{slot}").ar(tok); B(f"lhsSe{slot}").ar(tok)
            op(ES, cS, lambda g=g: ES.activation(L_sb[:, g * 1024:(g + 1) * 1024], PS[:, :], AF.Exp),
               dst=[B("L")], src=[bufPS])
        op(EV, cV, lambda: EV.tensor_tensor(L_sb[:, :].rearrange("s (h l) -> s h l", l=T),
                                            L_sb[:, :].rearrange("s (h l) -> s h l", l=T),
                                            G_sb[:, :].unsqueeze(1).broadcast_to([128, 32, T]),
                                            ALU.mult),
           dst=[B("L")], src=[B("L"), B("G")])
        if scut <= 3:
            return None
        for g in range(4):
            bufPY1.bw(cT, ET)
            B("xdt").br(cT, ET); B("L").br(cT, ET)
            ins = None
            for hh in range(8):
                h = g * 8 + hh
                ins = ET.matmul(PY1[(hh % 2) * 64:(hh % 2) * 64 + 64, (hh // 2) * 128:(hh // 2) * 128 + 128],
                                xdt_sb[:, h * HD:(h + 1) * HD], L_sb[:, h * T:(h + 1) * T],
                                start=True, stop=True)
            tokd = cT.inc(ins)
            bufPY1.aw(tokd); B("xdt").ar(tokd); B("L").ar(tokd)
            bufPY2.bw(cT, ET)
            B("Sst").br(cT, ET); BC_buf.br(cT, ET)
            ins = None
            for jj in range(4):
                tile = g * 4 + jj
                ins = ET.matmul(PY2[:, jj * 128:(jj + 1) * 128],
                                Sst[:, tile * 128:(tile + 1) * 128], CT_ap, start=True, stop=True)
            toko = cT.inc(ins)
            bufPY2.aw(toko); B("Sst").ar(toko); BC_buf.ar(toko)
            op(EV, cV, lambda g=g: EV.tensor_tensor(tmp_sb[:, g * 4:(g + 1) * 4, :],
                                                    PY2[:, :].rearrange("c (j t) -> c j t", t=T),
                                                    ecsrep_sb[:, g * 4:(g + 1) * 4, :], ALU.mult),
               dst=[B("tmp")], src=[bufPY2, B("ecsrep")])
            op(EV, cV, lambda g=g: EV.tensor_tensor(tmp_sb[:, g * 4:(g + 1) * 4, :],
                                                    tmp_sb[:, g * 4:(g + 1) * 4, :],
                                                    PY1[:, :].rearrange("c (j t) -> c j t", t=T), ALU.add),
               dst=[B("tmp")], src=[B("tmp"), bufPY1])
        op(EG, cG, lambda: EG.tensor_tensor(tmpD_sb[:, :, :], xT_ap,
                                            bc3(s_Drep[:, :], [128, 16, T]), ALU.mult),
           dst=[B("tmpD")], src=[xT_buf, B("s_Drep")])
        tY = op(EV, cV, lambda: EV.tensor_tensor(Y_sb[:, :, :], tmp_sb[:, :, :], tmpD_sb[:, :, :], ALU.add),
                dst=[B("Y")], src=[B("tmp"), B("tmpD")])
        if scut <= 4:
            return None
        tmpDf = tmpD_sb[:, :, :].rearrange("c j t -> c (j t)")
        for g in range(4):
            psum_st = PY1 if g % 2 == 0 else PY2
            buf_st = bufPY1 if g % 2 == 0 else bufPY2
            mms(psum_st[:, 0:512], [(Blh_sb[:, :], xdtds_sb[:, g * 512:(g + 1) * 512])],
                dst=[buf_st], src=[B("Blh"), B("xdtds")])
            op(EV, cV, lambda g=g: EV.tensor_tensor(
                   tmpDf[:, 0:512].rearrange("n (h p) -> n h p", p=HD),
                   Sst[:, g * 512:(g + 1) * 512].rearrange("n (h p) -> n h p", p=HD),
                   bc3(ecrep_sb[:, g * 8:(g + 1) * 8], [128, 8, HD]), ALU.mult),
               dst=[B("tmpD")], src=[B("Sst"), B("ecrep")])
            op(EV, cV, lambda g=g, psum_st=psum_st: EV.tensor_tensor(
                   Sst[:, g * 512:(g + 1) * 512], tmpDf[:, 0:512], psum_st[:, 0:512], ALU.add),
               dst=[B("Sst")], src=[B("tmpD"), buf_st])
        return tY

    def prologue():
        for dst_t, src_t, nm in [(s_csf, c_csf, "s_csf"), (s_csb, c_csb, "s_csb"),
                                 (s_dsf, c_dsf, "s_dsf"), (s_dsb, c_dsb, "s_dsb"),
                                 (s_ones, c_ones, "s_ones"), (s_idf, c_idf, "s_idf"),
                                 (s_mgf, c_mgf, "s_mg0"), (s_mgb, c_mgb, "s_mg1"),
                                 (s_idb, c_idb, "s_idb"), (s_nbi, c_nbi, "s_nbi")]:
            dma(dst_t[:, :], src_t[:, :], dst=[B(nm)])
        dma(s_utf[:, :], c_utf[:, :], dst=[B("s_utf")])
        dma(s_utb[:, :], c_utb[:, :], dst=[B("s_utb")])
        dma(s_rhsS[:, :], c_rhsS[:, :], dst=[B("rhsS_c")])
        op(EV, cV, lambda: EV.memset(lhsS[:, :, :], 0.0), dst=[B("lhsS0")])
        for sl in range(4):
            dma(lhsS[64 + 2 * sl:66 + 2 * sl, sl, :], c_ones2b[:, :],
                dst=[B(f"lhsSd{sl}")], src=[B("lhsS0")])
        dtb_b = bass.AP(tensor=dtb_d.tensor, offset=0, ap=[[0, 64], [0, 2], [1, NH]])
        dma(s_dtbrep[0:64, :].rearrange("p (a b) -> p a b", a=2), dtb_b, dst=[B("dtbrep")])
        dma(s_dtbrep[64:128, :].rearrange("p (a b) -> p a b", a=2), dtb_b, dst=[B("dtbrep")])
        dma(arow_sb[0:1, 64:96], bass.AP(tensor=Dv_d.tensor, offset=0, ap=[[0, 1], [1, NH]]),
            dst=[B("drow")])
        mms(PS[:, 0:32], [(s_ones[0:1, :], arow_sb[0:1, 64:96])],
            dst=[bufPS], src=[B("drow"), B("s_ones")])
        op(ES, cS, lambda: ES.copy(junk[:, 0:32], PS[:, 0:32]),
           dst=[B("junk")], src=[bufPS])
        op(EV, cV, lambda: EV.tensor_copy(
               s_Drep[0:64, :].unsqueeze(2),
               junk[0:64, 0:32].rearrange("p (a b) -> p a b", b=2)[:, :, 0:1]),
           dst=[B("s_Drep")], src=[B("junk")])
        op(EV, cV, lambda: EV.tensor_copy(
               s_Drep[64:128, :].unsqueeze(2),
               junk[64:128, 0:32].rearrange("p (a b) -> p a b", b=2)[:, :, 1:2]),
           dst=[B("s_Drep")], src=[B("junk")])
        dma(s_Dcol[:, :], bass.AP(tensor=Dv_d.tensor, offset=0, ap=[[1, NH], [1, 1]]),
            dst=[B("Dcol")])
        fb_b = bass.AP(tensor=fbias_d.tensor, offset=0, ap=[[0, 64], [1, D]])
        dma(s_biasrep[0:64, :], fb_b, dst=[B("biasrep")], q=EG)
        dma(s_biasrep[64:128, :], fb_b, dst=[B("biasrep")], q=EG)
        dma(arow_sb[0:1, 0:NH], bass.AP(tensor=Alog_d.tensor, offset=0, ap=[[0, 1], [1, NH]]),
            dst=[B("arow")])
        op(ES, cS, lambda: ES.activation(arow_sb[0:1, 32:64], arow_sb[0:1, 0:NH], AF.Exp),
           dst=[B("arow")], src=[B("arow")])
        op(EV, cV, lambda: EV.tensor_scalar(arow_sb[0:1, 32:64], arow_sb[0:1, 32:64], -1.0, None, ALU.mult),
           dst=[B("arow")], src=[B("arow")])
        dma(bass.AP(tensor=bn_misc.tensor, offset=0, ap=[[NH, 1], [1, NH]]),
            arow_sb[0:1, 32:64],
            src=[B("arow")], dst=[B("bnm")])
        ab_b = bass.AP(tensor=bn_misc.tensor, offset=0, ap=[[0, 64], [1, NH]])
        dma(s_Arep[0:64, :], ab_b, dst=[B("s_Arep")], src=[B("bnm")])
        dma(s_Arep[64:128, :], ab_b, dst=[B("s_Arep")], src=[B("bnm")])
        op(EV, cV, lambda: EV.memset(Sst[:, :], 0.0), dst=[B("Sst")])
        op(EV, cV, lambda: EV.memset(ybcar[:, :], 0.0), dst=[B("ybcar")])
        op(EV, cV, lambda: EV.memset(onescol_b[:, :], 1.0), dst=[B("onescol")])
        op(EV, cV, lambda: EV.memset(sqrow_b[:, :], float(np.sqrt(DI))), dst=[B("sqrow")])
        op(EV, cV, lambda: EV.memset(s_eps[:, 0:1], float(EPS)), dst=[B("s_eps")])
        op(EV, cV, lambda: EV.memset(s_eps[:, 1:2], float(EPS * DI)), dst=[B("s_eps")])


    p1_stk = contextlib.ExitStack()

    def p1_alloc():
        e = p1_stk.enter_context
        g = {}
        g["WinXB"] = e(nc.sbuf_tensor("WinXB", [128, 8, 2624], BF16))
        g["WfcD"] = e(nc.sbuf_tensor("WfcD", [128, 16, NH], BF16))
        g["convdiag"] = e(nc.sbuf_tensor("convdiag", [128, 100, 128], BF16))
        g["convw"] = e(nc.sbuf_tensor("convw_s", [128, 20, 4], F32))
        g["convbh"] = e(nc.sbuf_tensor("convbh_s", [128, 20], F32))
        g["spanA"] = e(nc.sbuf_tensor("spanA", [128, 1, 20, SPAN + 3], BF16))
        g["xsiluT"] = e(nc.sbuf_tensor("xsiluT", [128, SPANC, 20, 128], BF16))
        g["xBCpre"] = e(nc.sbuf_tensor("xBCpre", [128, 2560], BF16))
        g["onesp"] = e(nc.sbuf_tensor("onesp", [128, SPAN], BF16))
        g["fdT_sb"] = e(nc.sbuf_tensor("fdT_sb", [NH, 128], F32))
        g["fdTb_sb"] = e(nc.sbuf_tensor("fdTb_sb", [NH, 128], BF16))
        g["tnh"] = e(nc.sbuf_tensor("tnh", [128, 1024], BF16))

        return g

    def p1_prologue(g):
        for a in range(8):
            dma(junk[:, :], Wf_d.rearrange("(a p) o -> p a o", p=128)[:, a, :],
                dst=[B("junk")])
            quant(junk[:, :], [B("junk")], xd_sb[:, :], B("xd"), f"w{a}")
            dma(wdT[:, :, a * 128:(a + 1) * 128],
                xd_sb[:, :].rearrange("t (j c) -> t j c", c=128),
                src=[B("xd")], dst=[B("wdT")], transpose=True)
        dma(g["WinXB"][:, :, :], WinT_d.rearrange("(a p) o -> p a o", p=128)[:, :, 2048:4672],
            dst=[B("WinXB")], q=EG)
        dma(g["WfcD"][:, :, :], WfcDT_d.rearrange("(a p) o -> p a o", p=128),
            dst=[B("WfcD")], q=EG)
        dma(g["convw"][:, :, :], convw_d[:, :, :], dst=[B("convw")])
        dma(g["convbh"][:, :], convb_d[:, :], dst=[B("convbh")])
        op(EV, cV, lambda: EV.tensor_scalar(g["convbh"][:, :], g["convbh"][:, :], 0.5, None, ALU.mult),
           dst=[B("convbh")], src=[B("convbh")])
        for j in range(20):
            for tap in range(4):
                op(EV, cV, lambda j=j, tap=tap: EV.tensor_scalar(
                       g["convdiag"][:, j * 5 + tap, :], s_idb[:, :],
                       g["convw"][:, j, tap:tap + 1], None, ALU.mult),
                   dst=[B("convdiag")], src=[B("convw"), B("s_idb")])
            op(EV, cV, lambda j=j: EV.tensor_scalar(
                   g["convdiag"][:, j * 5 + 4, :], s_idb[:, :],
                   g["convbh"][:, j:j + 1], None, ALU.mult),
               dst=[B("convdiag")], src=[B("convbh"), B("s_idb")])
        for j in range(20):
            for tap in range(4):
                op(EV, cV, lambda j=j, tap=tap: EV.tensor_scalar(
                       g["convdiag"][:, j * 5 + tap, :], g["convdiag"][:, j * 5 + tap, :],
                       0.5, None, ALU.mult),
                   dst=[B("convdiag")], src=[B("convdiag")])
        op(EV, cV, lambda: EV.memset(g["onesp"][:, :], 1.0), dst=[B("onesp")])
        if "wdT" in dbg:
            for a in range(8):
                op(EV, cV, lambda a=a: EV.tensor_copy(junk[:, :], wdT[:, a, :]),
                   dst=[B("junk")], src=[B("wdT")])
                dma(dbg["wdT"][:, a, :], junk[:, :], src=[B("junk")])
        op(EV, cV, lambda: EV.memset(g["spanA"][:, 0, :, 0:3], 0.0), dst=[B("spanW0")])

    def p1_conv(g, k, sp):
        pass_halo = k + 1 < NCH
        for grp in range(5):
            psum = PA if grp % 2 == 0 else PB
            bufp = bufPA if grp % 2 == 0 else bufPB
            bufp.bw(cT, ET)
            B("spanW0").br(cT, ET); B("convdiag").br(cT, ET); B("onesp").br(cT, ET)
            ins = None
            for jj in range(4):
                j = grp * 4 + jj
                for tap in range(4):
                    ET.matmul(psum[:, jj * SPAN:(jj + 1) * SPAN],
                              g["convdiag"][:, j * 5 + tap, :],
                              g["spanA"][:, 0, j, tap:tap + SPAN],
                              start=(tap == 0), stop=False)
                ins = ET.matmul(psum[:, jj * SPAN:(jj + 1) * SPAN],
                                g["convdiag"][:, j * 5 + 4, :],
                                g["onesp"][:, :], start=False, stop=True)
            tok = cT.inc(ins)
            bufp.aw(tok)
            B("spanW0").ar(tok); B("convdiag").ar(tok); B("onesp").ar(tok)
            op(ES, cS, lambda psum=psum: ES.activation(g["tnh"][:, :], psum[:, :], AF.Tanh),
               dst=[B("tnh")], src=[bufp])
            for sc in range(SPANC):
                op(EV, cV, lambda grp=grp, psum=psum, sc=sc: EV.scalar_tensor_tensor(
                       g["xsiluT"][:, sc, grp * 4:(grp + 1) * 4, :],
                       g["tnh"][:, :].rearrange("c (j s t) -> c j s t", j=4, s=SPANC)[:, :, sc, :],
                       1.0, psum[:, :].rearrange("c (j s t) -> c j s t", j=4, s=SPANC)[:, :, sc, :],
                       ALU.add, ALU.mult),
                   dst=[B("xsilu")], src=[B("tnh"), bufp])
        if pass_halo:
            op(EV, cV, lambda: EV.tensor_copy(g["spanA"][:, 0, :, 0:3],
                                              g["spanA"][:, 0, :, SPAN:SPAN + 3]),
               dst=[B("spanW0")], src=[B("spanW0")])
        if "xBCs" in dbg:
            for sc in range(SPANC):
                for piece in range(5):
                    op(EV, cV, lambda piece=piece, sc=sc: EV.tensor_copy(
                           tmp_sb[:, 0:4, :].rearrange("c j t -> c (j t)")[:, 0:512],
                           g["xsiluT"][:, sc, piece * 4:(piece + 1) * 4, :].rearrange("c j t -> c (j t)")),
                       dst=[B("tmp")], src=[B("xsilu")])
                    kk = k - SPANC + 1 + sc
                    dma(dbg["xBCs"][:, piece * 4:(piece + 1) * 4, kk * T:(kk + 1) * T],
                        tmp_sb[:, 0:4, :],
                        src=[B("tmp")], q=EG, ks=str(sc))

    def p1_chunk(g, k):
        import os as _os
        CUT = int(_os.environ.get("MKCUT", "99"))
        chunk_dma_barrier()
        par = k % 2
        rows = slice(k * T, (k + 1) * T)
        dma(u_sb[:, par, :], u_d[rows, :], dst=[B(f"u{par}")])
        if CUT <= 0:
            if "sm" in dbg and k == 0:
                dma(dbg["sm"][:, 0:8], u_sb[:, par, 0:8], src=[B(f"u{par}")])
            return
        m1, r1, lnb = ln_stats(u_sb[:, par, :], 0, [B(f"u{par}")], "a")
        op(EV, cV, lambda: EV.tensor_scalar(h1_sb[:, :], u_sb[:, par, :], m1, None, ALU.subtract),
           dst=[B("h1")], src=[B(f"u{par}")] + lnb)
        op(EV, cV, lambda: EV.tensor_scalar(h1_sb[:, :], h1_sb[:, :], r1, None, ALU.mult),
           dst=[B("h1")], src=[B("h1")] + lnb)
        if "h1" in dbg:
            dma(dbg["h1"][rows, :], h1_sb[:, :], src=[B("h1")])
        if "sm" in dbg and k == 0:
            dma(junk[:, 0:8], u_d[0:128, 16:24], dst=[B("junk")])
            dma(junk[:, 8:12], convw_d[:, 0:4, 0].rearrange("p a -> p a") if False else convw_d[:, 0:1, :].rearrange("p a b -> p (a b)"), dst=[B("junk")])
            dma(junk[:, 12:16], WinT_d[0:128, 0:4], dst=[B("junk")])
            dma(dbg["sm"][:, 0:16], junk[:, 0:16], src=[B("junk")])
            dma(dbg["sm"][:, 16:32], u_sb[:, par, 0:16], src=[B("smra"), B(f"u{par}")])
        if CUT <= 1:
            return
        quant(h1_sb[:, :], [B("h1")], xd_sb[:, :], B("xd"), "f1")
        if "xd" in dbg:
            op(EV, cV, lambda: EV.tensor_copy(junk[:, :], xd_sb[:, :]), dst=[B("junk")], src=[B("xd")])
            dma(dbg["xd"][rows, :], junk[:, :], src=[B("junk")])
        if CUT <= 2:
            return
        dma(xdT_sb[:, :, :], xd_sb[:, :].rearrange("t (j c) -> t j c", c=128),
            src=[B("xd")], dst=[B("xdT")], transpose=True)
        if CUT <= 3:
            return
        for gg in range(2):
            mms(PA[:, gg * 512:(gg + 1) * 512],
                [(xdT_sb[:, kt, :], wdT[:, kt, gg * 512:(gg + 1) * 512]) for kt in range(8)],
                dst=[bufPA], src=[B("xdT"), B("wdT")])
        op(EV, cV, lambda: EV.tensor_tensor(junk[:, :], PA[:, :], s_biasrep[:, :], ALU.add),
           dst=[B("junk")], src=[bufPA, B("biasrep")])
        op(EV, cV, lambda: EV.scalar_tensor_tensor(x1_sb[:, :], junk[:, :], 0.5,
                                                   u_sb[:, par, :], ALU.mult, ALU.add),
           dst=[B("x1")], src=[B("junk"), B(f"u{par}")])
        if CUT <= 4:
            return
        dma(sp_x1[rows, :], x1_sb[:, :], src=[B("x1")])
        if "x1" in dbg:
            dma(dbg["x1"][rows, :], x1_sb[:, :], src=[B("x1")])
        m2, r2, lnb2 = ln_stats(x1_sb[:, :], 4, [B("x1")], "b")
        op(EV, cV, lambda: EV.tensor_scalar(junk[:, :], x1_sb[:, :], m2, None, ALU.subtract),
           dst=[B("junk")], src=[B("x1")] + lnb2)
        op(EV, cV, lambda: EV.tensor_scalar(h2b_sb[:, :], junk[:, :], r2, None, ALU.mult),
           dst=[B("h2b")], src=[B("junk")] + lnb2)
        if "h2" in dbg:
            op(EV, cV, lambda: EV.tensor_copy(junk[:, :], h2b_sb[:, :]), dst=[B("junk")], src=[B("h2b")])
            dma(dbg["h2"][rows, :], junk[:, :], src=[B("junk")])
        dma(h2T_sb[:, 0, :, :], h2b_sb[:, :].rearrange("t (j c) -> t j c", c=128),
            src=[B("h2b")], dst=[B("h2T")], transpose=True)
        dma(sp_h2T[:, :, rows], h2T_sb[:, 0, :, :], src=[B("h2T")])
        mms(PS[:, 0:64],
            [(h2T_sb[:, 0, kt, :], g["WinXB"][:, kt, 2560:2624]) for kt in range(8)],
            dst=[bufPS], src=[B("h2T"), B("WinXB")])
        op(EV, cV, lambda: EV.tensor_tensor(junk[:, 0:64], PS[:, 0:64], s_dtbrep[:, :], ALU.add),
           dst=[B("junk")], src=[bufPS, B("dtbrep")])
        op(ES, cS, lambda: ES.activation(junk[:, 64:128], junk[:, 0:64], AF.Exp),
           dst=[B("junkdt")], src=[B("junk")])
        op(ES, cS, lambda: ES.activation(dt_sb[:, par, :], junk[:, 64:128], AF.Ln, bias=1.0),
           dst=[B(f"dt{par}")], src=[B("junkdt")])
        dma(sp_dt[rows, :], dt_sb[:, par, :], src=[B(f"dt{par}")])
        if "dt" in dbg:
            dma(dbg["dt"][rows, :], dt_sb[:, par, :], src=[B(f"dt{par}")])
        for gg in range(5):
            psum = PA if gg % 2 == 0 else PB
            bufp = bufPA if gg % 2 == 0 else bufPB
            mms(psum[:, 0:512],
                [(h2T_sb[:, 0, kt, :], g["WinXB"][:, kt, gg * 512:(gg + 1) * 512]) for kt in range(8)],
                dst=[bufp], src=[B("h2T"), B("WinXB")])
            op(ES, cS, lambda gg=gg, psum=psum: ES.copy(g["xBCpre"][:, gg * 512:(gg + 1) * 512], psum[:, 0:512]),
               dst=[B("xBCpre")], src=[bufp])
        sp = 0
        off = 3 + (k % SPANC) * T
        h1b = h1_sb[:, :].bitcast(BF16)
        for jh in range(2):
            stg = h1b[:, 0:1280].rearrange("p (j c) -> p j c", c=128)
            dma(stg, g["xBCpre"][:, jh * 1280:(jh + 1) * 1280].rearrange("t (j c) -> t j c", c=128),
                src=[B("xBCpre")], dst=[B("h1")], transpose=True)
            dma(g["spanA"][:, 0, jh * 10:(jh + 1) * 10, off:off + T], stg,
                src=[B("h1")], dst=[B("spanW0")], ks="sa")

    def p1_back(g, k):
        par = k % 2
        rows = slice(k * T, (k + 1) * T)
        soff = (k % SPANC) * T
        dma(xog_sb[:, 0, :].rearrange("t (j c) -> t j c", c=128),
            g["xsiluT"][:, k % SPANC, 0:16, :],
            src=[B("xsilu")], dst=[B("xog")], transpose=True)
        dma(sp_xog[rows, :], xog_sb[:, 0, :], src=[B("xog")])
        dma(sp_xogT[:, :, rows], g["xsiluT"][:, k % SPANC, 0:16, :], src=[B("xsilu")], ks=str(par))
        dma(sp_BCb[:, :, rows], g["xsiluT"][:, k % SPANC, 18:20, :], src=[B("xsilu")], ks=str(par))
        mms(PS[0:32, 640:768],
            [(g["WfcD"][:, j, :], g["xsiluT"][:, k % SPANC, j, :]) for j in range(16)],
            dst=[bufPS], src=[B("WfcD"), B("xsilu")])
        op(EV, cV, lambda: EV.tensor_scalar(g["fdT_sb"][:, :], PS[0:32, 640:768],
                                            s_Dcol[:, :], None, ALU.add),
           dst=[B("fdT")], src=[bufPS, B("Dcol")])
        op(EV, cV, lambda: EV.tensor_copy(g["fdTb_sb"][:, :], g["fdT_sb"][:, :]),
           dst=[B("fdTb")], src=[B("fdT")])
        dma(sp_fdT[:, rows], g["fdTb_sb"][:, :], src=[B("fdTb")])
        import os as _os6
        lastp1 = (k == NCH - 1)
        scut1 = int(_os6.environ.get("MKSSD1CUT", "99")) if lastp1 else 99
        if k < int(_os6.environ.get("MKP1SSD", "99")):
            ssd(k, 0, par,
                g["xsiluT"][:, k % SPANC, 0:16, :], B("xsilu"),
                g["xsiluT"][:, k % SPANC, 16, :], g["xsiluT"][:, k % SPANC, 17, :], B("xsilu"),
                xog_sb[:, 0, :], B("xog"), scut=scut1)
        dma(sp_Yf[:, :, 1 + k * T: 1 + (k + 1) * T], Y_sb[:, :, :], src=[B("Y")])
        if "Yf" in dbg:
            dma(dbg["Yf"][:, :, rows], Y_sb[:, :, :], src=[B("Y")], q=EG)

    def p1_tail():
        drain_dmas(EQ)
        nc.all_engine_barrier()

    p2_stk = contextlib.ExitStack()

    def p2_alloc():
        e = p2_stk.enter_context
        g = {}
        g["WinZ"] = e(nc.sbuf_tensor("WinZ", [128, 8, 2048], BF16))
        g["WoT"] = e(nc.sbuf_tensor("WoTs", [128, 16, D], BF16))
        g["normw"] = e(nc.sbuf_tensor("normws", [128, 16], F32))
        g["Yfld"] = e(nc.sbuf_tensor("Yfld", [128, 16, 128], BF16))
        g["y2"] = e(nc.sbuf_tensor("y2", [128, 16, 128], BF16))
        g["fdrep"] = e(nc.sbuf_tensor("fdrep", [128, 16, 128], BF16))
        g["tz"] = e(nc.sbuf_tensor("tz", [128, DI], BF16))
        g["zh"] = e(nc.sbuf_tensor("zh", [128, DI], BF16))
        g["tzT"] = e(nc.sbuf_tensor("tzT", [128, 16, 128], BF16))
        g["zhT"] = e(nc.sbuf_tensor("zhT", [128, 16, 128], BF16))
        g["x2"] = e(nc.sbuf_tensor("x2s", [128, D], F32))
        g["xogT"] = e(nc.sbuf_tensor("xogTl", [128, 16, 128], BF16))
        g["BC2"] = e(nc.sbuf_tensor("BC2", [128, 1, 2, 128], BF16))
        return g

    def p2_prologue(g):
        dma(g["WinZ"][:, :, :], WinT_d.rearrange("(a p) o -> p a o", p=128)[:, :, 0:2048],
            dst=[B("WinZ")], q=EG)
        dma(g["WoT"][:, :, :], WoT_d.rearrange("(a p) o -> p a o", p=128),
            dst=[B("WoTr")], q=EG)
        dma(g["normw"][:, :], normw_d[:, :], dst=[B("normw")])
        op(EV, cV, lambda: EV.tensor_tensor(g["WoT"][:, :, :], g["WoT"][:, :, :],
                                            bc3(g["normw"][:, :], [128, 16, D]), ALU.mult),
           dst=[B("WoTr")], src=[B("WoTr"), B("normw")])
        op(EV, cV, lambda: EV.memset(Sst[:, :], 0.0), dst=[B("Sst")])
        op(EV, cV, lambda: EV.memset(ybcar[:, :], 0.0), dst=[B("ybcar")])

    def p2_chunk(g, k):
        import os as _os4
        P2CUT = int(_os4.environ.get("MKP2CUT", "99"))
        chunk_dma_barrier()
        last = (k == 0)
        par = k % 2
        rows = slice(k * T, (k + 1) * T)
        dma(u_sb[:, par, :], sp_x1[rows, :], dst=[B(f"u{par}")])
        dma(h2T_sb[:, 0, :, :], sp_h2T[:, :, rows], dst=[B("h2T")])
        dma(xog_sb[:, 0, :], sp_xog[rows, :], dst=[B("xog")])
        dma(g["xogT"][:, :, :], sp_xogT[:, :, rows], dst=[B("xogT")])
        dma(g["BC2"][:, 0, :, :], sp_BCb[:, :, rows], dst=[B("BC2")])
        dma(dt_sb[:, par, :], sp_dt[rows, :], dst=[B(f"dt{par}")])
        if k == 0:
            op(EV, cV, lambda: EV.memset(g["Yfld"][:, :, 0:1], 0.0), dst=[B("Yfld")])
            dma(g["Yfld"][:, :, 1:T], sp_Yf[:, :, 1:T], dst=[B("Yfld")])
        else:
            dma(g["Yfld"][:, :, :], sp_Yf[:, :, k * T:(k + 1) * T], dst=[B("Yfld")])
        for half in range(2):
            fb = bass.AP(tensor=sp_fdT.tensor, offset=half * L + k * T,
                         ap=[[0, 64], [2 * L, 16], [1, T]])
            dma(g["fdrep"][half * 64:(half + 1) * 64, :, :], fb, dst=[B("fdrep")])
        if last and P2CUT <= 1:
            return
        import os as _os5
        scut = int(_os5.environ.get("MKSSDCUT", "99")) if last else 99
        ssd(k, 1, par,
            g["xogT"][:, :, :], B("xogT"),
            g["BC2"][:, 0, 0, :], g["BC2"][:, 0, 1, :], B("BC2"),
            xog_sb[:, 0, :], B("xog"), scut=scut)
        if "Yb" in dbg:
            dma(dbg["Yb"][:, :, rows], Y_sb[:, :, :], src=[B("Y")], q=EG)
        op(EV, cV, lambda: EV.tensor_tensor(g["y2"][:, :, 0:127],
                                            Y_sb[:, :, 1:128], g["Yfld"][:, :, 0:127], ALU.add),
           dst=[B("y2")], src=[B("Y"), B("Yfld")])
        op(EV, cV, lambda: EV.tensor_tensor(g["y2"][:, :, 127:128],
                                            ybcar[:, :].unsqueeze(2),
                                            g["Yfld"][:, :, 127:128], ALU.add),
           dst=[B("y2")], src=[B("ybcar"), B("Yfld")])
        op(EV, cV, lambda: EV.tensor_copy(ybcar[:, :],
                                          Y_sb[:, :, 0:1].rearrange("c j t -> c (j t)")),
           dst=[B("ybcar")], src=[B("Y")])
        op(EG, cG, lambda: EG.tensor_tensor(tmpD_sb[:, :, :], g["xogT"][:, :, :],
                                            g["fdrep"][:, :, :], ALU.mult),
           dst=[B("tmpD")], src=[B("xogT"), B("fdrep")])
        op(EV, cV, lambda: EV.tensor_tensor(g["y2"][:, :, :], g["y2"][:, :, :],
                                            tmpD_sb[:, :, :], ALU.add),
           dst=[B("y2")], src=[B("y2"), B("tmpD")])
        if "y" in dbg:
            dma(dbg["y"][:, :, rows], g["y2"][:, :, :], src=[B("y2")], q=EG)
        if last and P2CUT <= 2:
            return
        for gg in range(2):
            psum = PA if gg == 0 else PB
            bufp = bufPA if gg == 0 else bufPB
            for i in range(2):
                mms(psum[:, i * 512:(i + 1) * 512],
                    [(h2T_sb[:, 0, kt, :],
                      g["WinZ"][:, kt, gg * 1024 + i * 512: gg * 1024 + (i + 1) * 512])
                     for kt in range(8)],
                    dst=[bufp], src=[B("h2T"), B("WinZ")])
            op(ES, cS, lambda gg=gg, psum=psum: ES.activation(g["tz"][:, gg * 1024:(gg + 1) * 1024],
                                                              psum[:, :], AF.Tanh, scale=0.5),
               dst=[B("tz")], src=[bufp])
            op(ES, cS, lambda gg=gg, psum=psum: ES.activation(g["zh"][:, gg * 1024:(gg + 1) * 1024],
                                                              psum[:, :], AF.Copy, scale=0.5),
               dst=[B("zh")], src=[bufp])
        dma(g["tzT"][:, :, :], g["tz"][:, :].rearrange("t (j c) -> t j c", c=128),
            src=[B("tz")], dst=[B("tzT")], transpose=True)
        dma(g["zhT"][:, :, :], g["zh"][:, :].rearrange("t (j c) -> t j c", c=128),
            src=[B("zh")], dst=[B("zhT")], transpose=True)
        op(EV, cV, lambda: EV.tensor_tensor(tmp_sb[:, :, :], g["y2"][:, :, :],
                                            g["zhT"][:, :, :], ALU.mult),
           dst=[B("tmp")], src=[B("y2"), B("zhT")])
        op(EV, cV, lambda: EV.scalar_tensor_tensor(g["y2"][:, :, :], g["tzT"][:, :, :],
                                                   1.0, tmp_sb[:, :, :], ALU.add, ALU.mult),
           dst=[B("y2")], src=[B("tzT"), B("tmp")])
        if last and P2CUT <= 3:
            return
        op(ES, cS, lambda: ES.activation(tmpD_sb[:, :, :], g["y2"][:, :, :], AF.Square),
           dst=[B("tmpD")], src=[B("y2")])
        bufPS.bw(cT, ET)
        B("tmpD").br(cT, ET); B("onescol").br(cT, ET)
        ins = None
        for j in range(16):
            ins = ET.matmul(PS[0:1, 0:128], onescol_b[:, :], tmpD_sb[:, j, :],
                            start=(j == 0), stop=(j == 15))
        tokn = cT.inc(ins)
        bufPS.aw(tokn); B("tmpD").ar(tokn); B("onescol").ar(tokn)
        op(ES, cS, lambda: ES.activation(junk[0:1, 0:128], PS[0:1, 0:128], AF.Ln, bias=s_eps[0:1, 1:2]),
           dst=[B("rowl"), B("junk")], src=[bufPS, B("s_eps")])
        op(ES, cS, lambda: ES.activation(junk[0:1, 0:128], junk[0:1, 0:128], AF.Exp, scale=-0.5),
           dst=[B("rowr"), B("junk")], src=[B("rowl")])
        op(EV, cV, lambda: EV.tensor_copy(rowb_sb[:, :], junk[0:1, 0:128]),
           dst=[B("rowb")], src=[B("rowr"), B("junk")])
        bufPS.bw(cT, ET)
        B("rowb").br(cT, ET); B("sqrow").br(cT, ET)
        ins = ET.matmul(PS[:, 512:640], sqrow_b[:, :], rowb_sb[:, :], start=True, stop=True)
        tokr = cT.inc(ins)
        bufPS.aw(tokr); B("rowb").ar(tokr); B("sqrow").ar(tokr)
        yn_ap = xdt_sb[:, :].rearrange("c (j t) -> c j t", t=128)
        op(EV, cV, lambda: EV.tensor_tensor(yn_ap, g["y2"][:, :, :],
                                            PS[:, 512:640].unsqueeze(1).broadcast_to([128, 16, 128]),
                                            ALU.mult),
           dst=[B("xdt")], src=[B("y2"), bufPS])
        if "yn" in dbg:
            op(EV, cV, lambda: EV.tensor_copy(Y_sb[:, :, :], yn_ap),
               dst=[B("Y")], src=[B("xdt")])
            dma(dbg["yn"][:, :, rows], Y_sb[:, :, :], src=[B("Y")], q=EG)
        if last and P2CUT <= 4:
            return
        for gg in range(2):
            mms(PA[:, gg * 512:(gg + 1) * 512],
                [(yn_ap[:, j, :], g["WoT"][:, j, gg * 512:(gg + 1) * 512]) for j in range(16)],
                dst=[bufPA], src=[B("xdt"), B("WoTr")])
        op(EV, cV, lambda: EV.tensor_tensor(g["x2"][:, :], u_sb[:, par, :], PA[:, :], ALU.add),
           dst=[B("x2b")], src=[B(f"u{par}"), bufPA])
        if "x2" in dbg:
            dma(dbg["x2"][rows, :], g["x2"][:, :], src=[B("x2b")])
        if last and P2CUT <= 5:
            return
        m3, r3, lnb3 = ln_stats(g["x2"][:, :], 8, [B("x2b")], "c")
        op(EV, cV, lambda: EV.tensor_scalar(h1_sb[:, :], g["x2"][:, :], m3, None, ALU.subtract),
           dst=[B("h1")], src=[B("x2b")] + lnb3)
        op(EV, cV, lambda: EV.tensor_scalar(h1_sb[:, :], h1_sb[:, :], r3, None, ALU.mult),
           dst=[B("h1")], src=[B("h1")] + lnb3)
        quant(h1_sb[:, :], [B("h1")], xd_sb[:, :], B("xd"), "f2")
        dma(xdT_sb[:, :, :], xd_sb[:, :].rearrange("t (j c) -> t j c", c=128),
            src=[B("xd")], dst=[B("xdT")], transpose=True)
        for gg in range(2):
            mms(PB[:, gg * 512:(gg + 1) * 512],
                [(xdT_sb[:, kt, :], wdT[:, kt, gg * 512:(gg + 1) * 512]) for kt in range(8)],
                dst=[bufPB], src=[B("xdT"), B("wdT")])
        op(EV, cV, lambda: EV.tensor_tensor(junk[:, :], PB[:, :], s_biasrep[:, :], ALU.add),
           dst=[B("junk")], src=[bufPB, B("biasrep")])
        op(EV, cV, lambda: EV.scalar_tensor_tensor(x1_sb[:, :], junk[:, :], 0.5,
                                                   g["x2"][:, :], ALU.mult, ALU.add),
           dst=[B("x1")], src=[B("junk"), B("x2b")])
        dma(out_d[rows, :], x1_sb[:, :], src=[B("x1")])

    def finish():
        drain_dmas(EQ)

    with nc.Block() as blk:
        @blk.sync
        def _(sync):
            if "sm" in dbg and _os2.environ.get("MKPRED"):
                dma(x1_sb[:, 0:8], u_d[0:128, 0:8], dst=[B("x1pre")])
                dma(dbg["sm"][:, 0:8], x1_sb[:, 0:8], src=[B("x1pre")])
            prologue()
            for _eng in (EV, ES, ET, EG, EQ):
                drain_dmas(_eng)
            if "sm" in dbg and _os2.environ.get("MKPRED"):
                dma(x1_sb[:, 8:16], u_d[0:128, 8:16], dst=[B("x1pre")])
                dma(dbg["sm"][:, 8:16], x1_sb[:, 8:16], src=[B("x1pre")])
            g1 = p1_alloc()
            if "sm" in dbg and _os2.environ.get("MKPRED"):
                dma(x1_sb[:, 16:24], u_d[0:128, 24:32], dst=[B("x1pre")])
                dma(dbg["sm"][:, 16:24], x1_sb[:, 16:24], src=[B("x1pre")])
            p1_prologue(g1)
            if "sm" in dbg and _os2.environ.get("MKPRED"):
                dma(x1_sb[:, 24:32], u_d[0:128, 24:32], dst=[B("x1pre")])
                dma(dbg["sm"][:, 24:32], x1_sb[:, 24:32], src=[B("x1pre")])
            for s in range(NCH // SPANC):
                for kc in range(SPANC):
                    p1_chunk(g1, s * SPANC + kc)
                if not front_only:
                    p1_conv(g1, s * SPANC + SPANC - 1, 0)
                    for kc in range(SPANC):
                        p1_back(g1, s * SPANC + kc)
            p1_tail()
            p1_stk.close()
            if not p1_only:
                import os as _os3
                p2n = int(_os3.environ.get("MKP2N", str(NCH)))
                g2 = p2_alloc()
                p2_prologue(g2)
                for k in range(NCH - 1, max(-1, NCH - 1 - p2n), -1):
                    p2_chunk(g2, k)
            finish()
    return nc


def host_prep(inputs, core):
    m = {
        "u": inputs["L0_attn_u"][core],
        "WinT": np.ascontiguousarray(inputs["L0_attn_in_proj_w"].T),
        "WoT": np.ascontiguousarray(inputs["L0_attn_out_proj_w"].T),
        "WfcDT": np.ascontiguousarray(inputs["L0_attn_fc_D_w"].T),
        "Wf": inputs["L0_ffn_weight"],
        "convw": np.ascontiguousarray(
            inputs["L0_attn_conv_w"][:, 0, :].reshape(20, 128, 4).transpose(1, 0, 2)),
        "convb": np.ascontiguousarray(inputs["L0_attn_conv_b"].reshape(20, 128).T),
        "dtb": inputs["L0_attn_dt_bias"],
        "Alog": inputs["L0_attn_A_log"],
        "Dv": inputs["L0_attn_D"],
        "normw": np.ascontiguousarray(inputs["L0_attn_norm_w"].reshape(16, 128).T),
        "fbias": inputs["L0_ffn_bias"],
    }
    return {k: np.ascontiguousarray(np.asarray(v, dtype=np.float32)) for k, v in m.items()}


def kernel(**inputs):
    nc = build()
    ms = [host_prep(inputs, c) for c in range(8)]
    res = run_bass_kernel_spmd(nc, ms, core_ids=list(range(8)))
    return np.stack([np.asarray(res.results[c]["out"]) for c in range(8)]).astype(np.float32)
